# version 18
# baseline (speedup 1.0000x reference)
import numpy as np
from contextlib import ExitStack

import concourse.bass as bass
import concourse.mybir as mybir
from concourse.bass_utils import run_bass_kernel_spmd

F32 = mybir.dt.float32
BF16 = mybir.dt.bfloat16
AF = mybir.ActivationFunctionType
OP = mybir.AluOpType

S = 2048
E = 1024
H = 16
NHP = 6
ISCALE = float(1.0 / np.sqrt(32.0))
HSCALE = float(np.sqrt(32.0))
NB2 = 62.0


class Cnt:
    __slots__ = ("v",)
    def __init__(self):
        self.v = 0
    def inc(self, n=1):
        self.v += n
        return self.v


def build_program():
    nc = bass.Bass("TRN2", target_bir_lowering=False, detect_race_conditions=False)

    x_h = nc.dram_tensor("x", [S, E], F32, kind="ExternalInput").ap()
    wq_h = nc.dram_tensor("wq", [E, E], F32, kind="ExternalInput").ap()
    wv_h = nc.dram_tensor("wv", [E, E], F32, kind="ExternalInput").ap()
    wc_h = nc.dram_tensor("wc", [E, 16384], F32, kind="ExternalInput").ap()
    bqpp_h = nc.dram_tensor("bq_pp", [128, 8], F32, kind="ExternalInput").ap()
    bvbc_h = nc.dram_tensor("bv_bc", [128, E], F32, kind="ExternalInput").ap()
    hpd_h = nc.dram_tensor("hp_dup", [128, NHP], F32, kind="ExternalInput").ap()
    hpb_h = nc.dram_tensor("hp_bias", [NHP, 1], F32, kind="ExternalInput").ap()
    p2x_h = nc.dram_tensor("pow2x", [NHP, 128], F32, kind="ExternalInput").ap()
    p2c_h = nc.dram_tensor("pow2c", [NHP, 1], F32, kind="ExternalInput").ap()
    id_h = nc.dram_tensor("ident", [128, 128], F32, kind="ExternalInput").ap()
    rm4_h = nc.dram_tensor("rm4", [64, 64 * 64], F32, kind="ExternalInput").ap()
    up4_h = nc.dram_tensor("up4", [64, 64 * 128], F32, kind="ExternalInput").ap()
    out_h = nc.dram_tensor("out", [S, E], F32, kind="ExternalOutput").ap()
    wqbf = nc.dram_tensor("wq_bf", [E, E], BF16, kind="Internal").ap()
    wvbf = nc.dram_tensor("wv_bf", [E, E], BF16, kind="Internal").ap()
    wcbf = nc.dram_tensor("wc_bf", [E, 16384], BF16, kind="Internal").ap()
    x1scr = nc.dram_tensor("x1_scr", [S, E], F32, kind="Internal").ap()
    attnscr = nc.dram_tensor("attn_scr", [65, H * S], BF16, kind="Internal").ap()

    Qs = {k: [] for k in "TVSGY"}
    def em(q, f):
        Qs[q].append(f)

    with ExitStack() as ctx:
        A = lambda nm, sh, dt: ctx.enter_context(nc.sbuf_tensor(nm, sh, dt))
        dS = ctx.enter_context(nc.semaphore("dS"))
        dG = ctx.enter_context(nc.semaphore("dG"))
        ts = ctx.enter_context(nc.semaphore("ts"))
        vs = ctx.enter_context(nc.semaphore("vs"))
        ss = ctx.enter_context(nc.semaphore("ss"))
        gs_sem = ctx.enter_context(nc.semaphore("gs"))
        cS, cG, cT, cV, cSS, cGP = Cnt(), Cnt(), Cnt(), Cnt(), Cnt(), Cnt()
        SEM = {"T": (ts, cT), "V": (vs, cV), "S": (ss, cSS), "G": (gs_sem, cGP)}

        def wt(q, sem, val):
            if val is None:
                return
            em(q, lambda e, s=sem, v=val: e.wait_ge(s, v))

        def dma(out, in_, transpose=False, swdge=False):
            if swdge:
                n = cG.inc(16)
                em("G", lambda e, o=out, i=in_: e.dma_start(out=o, in_=i).then_inc(dG, 16))
            else:
                n = cS.inc(16)
                em("Y", lambda e, o=out, i=in_, t=transpose: e.dma_start(
                    out=o, in_=i, transpose=t).then_inc(dS, 16))
            return n

        def op(q, f, inc=False):
            if not inc:
                em(q, f)
                return None
            sem, cnt = SEM[q]
            n = cnt.inc()
            em(q, lambda e, f=f, s=sem: f(e).then_inc(s, 1))
            return n

        PS = ctx.enter_context(nc.psum_tensor("PS", [128, 2048], F32))
        PC = ctx.enter_context(nc.psum_tensor("PC", [128, 512], F32))
        PT2 = ctx.enter_context(nc.psum_tensor("PT2", [128, 1024], BF16))
        PT = ctx.enter_context(nc.psum_tensor("PT", [128, 1024], BF16))
        AV2 = ctx.enter_context(nc.psum_tensor("AV2", [128, 512], F32))

        identb = A("identb", [128, 128], BF16)
        pow2xb = A("pow2xb", [NHP, 128], BF16)
        pow2cb = A("pow2cb", [NHP, 1], BF16)
        hpdf = A("hpdf", [128, NHP], F32)
        hpsb = A("hpsb", [128, NHP], BF16)
        hpbias = A("hpbias", [NHP, 1], F32)
        bqpp = A("bqpp", [128, 8], F32)
        bvbcb = A("bvbcb", [128, E], BF16)
        W2S = A("W2S", [128, 8 * 1024], BF16)
        xt = [A("xt0", [128, E], F32), A("xt1", [128, E], F32)]
        x1t = [A("x1t0", [128, E], F32), A("x1t1", [128, E], F32)]
        lnA = A("lnA", [128, E], F32)
        xn2t = [A("xn2t0", [128, E], BF16), A("xn2t1", [128, E], BF16)]
        stats = A("stats", [128, 12], F32)
        aggr = A("aggr", [128, 2], F32)
        msc = A("msc", [128, 8], F32)

        state = {"msc_reader_s": None, "lnA_reader_v": None}

        def ln_chain(src_ap):
            wt("V", ss, state["msc_reader_s"])
            op("V", lambda e: e.bn_stats(stats[:, 0:6], src_ap[:, 0:512]))
            n1 = op("V", lambda e: e.bn_stats(stats[:, 6:12], src_ap[:, 512:1024]), inc=True)
            wt("V", vs, n1)
            n2 = op("V", lambda e: e.bn_aggr(aggr[:], stats[:]), inc=True)
            wt("V", vs, n2)
            op("V", lambda e: e.tensor_scalar(msc[:, 0:1], aggr[:, 0:1], -1.0, None, OP.mult))
            n3 = op("V", lambda e: e.tensor_scalar(msc[:, 1:2], aggr[:, 1:2], 1e-5, None, OP.add), inc=True)
            wt("V", vs, n3)
            n4 = op("V", lambda e: e.reciprocal(msc[:, 1:2], msc[:, 1:2]), inc=True)
            wt("S", vs, n4)
            n5 = op("S", lambda e: e.activation(msc[:, 2:3], msc[:, 1:2], AF.Sqrt), inc=True)
            wt("S", ss, n5)
            n6 = op("S", lambda e: e.mul(msc[:, 3:4], msc[:, 0:1], msc[:, 2:3]), inc=True)
            wt("S", ss, n6)

        def ln_apply(dst_ap, src_ap):
            n = op("S", lambda e: e.activation(
                dst_ap, src_ap, AF.Identity, bias=msc[:, 3:4], scale=msc[:, 2:3]), inc=True)
            state["msc_reader_s"] = n
            return n

        c01 = ExitStack()
        rm4b = c01.enter_context(nc.sbuf_tensor("rm4b", [64, 64 * 64], BF16))
        up4b = c01.enter_context(nc.sbuf_tensor("up4b", [64, 64 * 128], BF16))
        dma(identb[:], id_h[:], swdge=True)
        dma(rm4b[:], rm4_h[:], swdge=True)
        nG_rm = dma(up4b[:], up4_h[:], swdge=True)
        dma(wqbf[:], wq_h[:], swdge=True)
        nG_wqv = dma(wvbf[:], wv_h[:], swdge=True)
        dma(pow2xb[:], p2x_h[:], swdge=True)
        dma(pow2cb[:], p2c_h[:], swdge=True)
        nG_consts = dma(bvbcb[:], bvbc_h[:], swdge=True)
        dma(hpdf[:], hpd_h[:])
        dma(hpbias[:], hpb_h[:])
        nY_consts = dma(bqpp[:], bqpp_h[:])
        wt("V", dS, nY_consts)
        op("V", lambda e: e.tensor_scalar(hpsb[:], hpdf[:], HSCALE, None, OP.mult), inc=True)
        nG_wc = nG_consts
        for i in range(16):
            nG_wc = dma(wcbf[64 * i:64 * (i + 1), :], wc_h[64 * i:64 * (i + 1), :], swdge=True)

        wt("T", dG, nG_rm)
        w2_evac = {}
        for c in range(128):
            g, j = c // 8, c % 8
            beta = 64 * (g % 2)
            P = PC if (c % 2 == 0) else AV2
            if c >= 2:
                wt("T", vs, w2_evac[c - 2])
            rb = 32 * (c % 2)
            nT = op("T", lambda e, c=c, P=P, beta=beta, rb=rb: e.matmul(
                P[beta:beta + 64, 0:128],
                rm4b[rb:rb + 32, (c // 2) * 64:(c // 2) * 64 + 64],
                up4b[rb:rb + 32, (c // 2) * 128:(c // 2) * 128 + 128],
                start=True, stop=True), inc=True)
            wt("V", ts, nT)
            w2_evac[c] = op("V", lambda e, P=P, beta=beta, g=g, j=j: e.tensor_copy(
                W2S[beta:beta + 64, (g // 2) * 1024 + j * 128:(g // 2) * 1024 + (j + 1) * 128],
                P[beta:beta + 64, 0:128]), inc=True)
        c01.close()

        if True:
            with ExitStack() as cqv:
                qsT = cqv.enter_context(nc.sbuf_tensor("qsT", [128, 8 * S], BF16))
                vhat = cqv.enter_context(nc.sbuf_tensor("vhat", [128, 256 * 65], BF16))

                with ExitStack() as cx1:
                    xn1T = cx1.enter_context(nc.sbuf_tensor("xn1T", [128, 8 * S], BF16))
                    xn1t = [cx1.enter_context(nc.sbuf_tensor("xn1t0", [128, E], BF16)),
                            cx1.enter_context(nc.sbuf_tensor("xn1t1", [128, E], BF16))]
                    ln1_apply, ln1_ready = {}, {}
                    for tc in range(16):
                        b = tc % 2
                        if tc >= 2:
                            wt("Y", ss, ln1_apply[tc - 2])
                        nload = dma(xt[b][:], x_h[tc * 128:(tc + 1) * 128, :])
                        wt("V", dS, nload)
                        ln_chain(xt[b].ap())
                        if tc >= 2:
                            wt("S", ts, ln1_ready[tc - 2][0])
                        napp = ln_apply(xn1t[b][:], xt[b][:])
                        ln1_apply[tc] = napp
                        wt("T", ss, napp)
                        Ptc = PT if tc % 2 == 0 else PT2
                        if tc >= 2:
                            wt("T", vs, ln1_ready[tc - 2][1])
                        nT = None
                        for ec in range(8):
                            nT = op("T", lambda e, b=b, ec=ec, Ptc=Ptc: e.transpose(
                                Ptc[:, ec * 128:(ec + 1) * 128],
                                xn1t[b][:, ec * 128:(ec + 1) * 128], identb[:]),
                                inc=(ec == 7))
                        wt("V", ts, nT)
                        xv = xn1T.ap().rearrange("p (ec t) -> p ec t", ec=8)
                        nev = op("V", lambda e, tc=tc, xv=xv, Ptc=Ptc: e.tensor_copy(
                            xv[:, :, tc * 128:(tc + 1) * 128],
                            Ptc.ap().rearrange("p (ec u) -> p ec u", ec=8)), inc=True)
                        ln1_ready[tc] = (nT, nev)
                    ln1_done_v = ln1_ready[15][1]

                    wt("Y", dG, nG_wqv)
                    with ExitStack() as cw:
                        wqT = cw.enter_context(nc.sbuf_tensor("wqT", [128, 8 * E], BF16))
                        nw = None
                        for ec in range(8):
                            nw = dma(wqT[:, ec * E:(ec + 1) * E],
                                     wqbf[0:E, ec * 128:(ec + 1) * 128], transpose=True)
                        wt("T", dS, nw)
                        wt("T", vs, ln1_done_v)
                        qv_ev = {}
                        gi = 0
                        for oc in range(8):
                            for tq in range(4):
                                half = gi % 4
                                if gi >= 4:
                                    wt("T", vs, qv_ev[gi - 4])
                                nT = None
                                for ec in range(8):
                                    nT = op("T", lambda e, oc=oc, tq=tq, ec=ec, half=half: e.matmul(
                                        PS[:, half * 512:half * 512 + 512],
                                        wqT[:, ec * E + oc * 128: ec * E + (oc + 1) * 128],
                                        xn1T[:, ec * S + tq * 512: ec * S + tq * 512 + 512],
                                        start=(ec == 0), stop=(ec == 7)), inc=(ec == 7))
                                wt("V", ts, nT)
                                qv_ev[gi] = op("V", lambda e, oc=oc, tq=tq, half=half: e.tensor_scalar(
                                    qsT[:, oc * S + tq * 512: oc * S + tq * 512 + 512],
                                    PS[:, half * 512:half * 512 + 512],
                                    bqpp[:, oc:oc + 1], ISCALE, OP.add, OP.mult), inc=True)
                                gi += 1
                        q_last_v = qv_ev[31]
                    with ExitStack() as cw:
                        wvT = cw.enter_context(nc.sbuf_tensor("wvT", [128, 8 * E], BF16))
                        wt("Y", vs, q_last_v)
                        nw = None
                        for ec in range(8):
                            nw = dma(wvT[:, ec * E:(ec + 1) * E],
                                     wvbf[0:E, ec * 128:(ec + 1) * 128], transpose=True)
                        wt("T", dS, nw)
                        wt("V", dG, nG_consts)
                        vv = vhat.ap().rearrange("p (th d) -> p th d", d=65)
                        v_ev = {}
                        gi = 0
                        for tc in range(16):
                            for ov in range(2):
                                half = gi % 4
                                if gi >= 4:
                                    wt("T", vs, v_ev[gi - 4])
                                nT = None
                                for ec in range(8):
                                    nT = op("T", lambda e, tc=tc, ov=ov, ec=ec, half=half: e.matmul(
                                        PS[:, half * 512:half * 512 + 512],
                                        xn1T[:, ec * S + tc * 128: ec * S + (tc + 1) * 128],
                                        wvT[:, ec * E + ov * 512: ec * E + ov * 512 + 512],
                                        start=(ec == 0), stop=(ec == 7)), inc=(ec == 7))
                                wt("V", ts, nT)
                                v_ev[gi] = op("V", lambda e, tc=tc, ov=ov, half=half, vv=vv: e.tensor_tensor(
                                    vv[:, tc * 16 + 8 * ov: tc * 16 + 8 * ov + 8, 0:64],
                                    PS[:, half * 512:half * 512 + 512].rearrange("p (h d) -> p h d", h=8),
                                    bvbcb[:, ov * 512:ov * 512 + 512].rearrange("p (h d) -> p h d", h=8),
                                    OP.add), inc=True)
                                gi += 1
                        n_ones = op("V", lambda e, vv=vv: e.memset(vv[:, :, 64:65], 1.0), inc=True)

                with ExitStack() as ca:
                    Aa = lambda nm, sh, dt: ca.enter_context(nc.sbuf_tensor(nm, sh, dt))
                    bbc = [Aa("bbc0", [128, S], BF16), Aa("bbc1", [128, S], BF16)]
                    bkT = [Aa("bkT0", [128, 16], F32), Aa("bkT1", [128, 16], F32)]
                    bitsb = Aa("bitsb", [NHP, S], BF16)
                    coefb = [Aa("coef0", [128, 2048], BF16), Aa("coef1", [128, 2048], BF16)]
                    Wb = [Aa("Wb0", [128, S], F32), Aa("Wb1", [128, S], F32)]
                    Ugrp = Aa("Ugrp", [128, 4 * S], BF16)
                    UTs = Aa("UTs", [128, 16 * 256], BF16)
                    gmx = Aa("gmx", [128, 6], F32)
                    avsb = [Aa("avsb0", [65, 256], BF16), Aa("avsb1", [65, 256], BF16)]

                    wmult_prev = {0: None, 1: None}
                    exp_prev = {}
                    wb_reader_s = {0: None, 1: None}
                    transp_last = {0: None, 1: None}
                    last_prep_banks = None

                    prep_banks_v = {"n": None}
                    def emit_prep(h):
                        hb = h % 2
                        beta = 64 * hb
                        ch = h // 2
                        bits_ev = {}
                        for qc in range(4):
                            if qc == 0:
                                wt("T", vs, prep_banks_v["n"])
                                wt("T", vs, wmult_prev[0])
                            else:
                                wt("T", vs, bits_ev[qc - 1])
                            nT = op("T", lambda e, qc=qc, beta=beta, ch=ch: e.matmul(
                                PS[0:NHP, 0:512], hpsb[beta:beta + 64, :],
                                qsT[beta:beta + 64, ch * S + qc * 512: ch * S + qc * 512 + 512],
                                start=True, stop=True), inc=True)
                            wt("V", ts, nT)
                            bits_ev[qc] = op("V", lambda e, qc=qc: e.tensor_scalar(
                                bitsb[:, qc * 512:(qc + 1) * 512], PS[0:NHP, 0:512],
                                hpbias[:], 0.0, OP.add, OP.is_ge), inc=True)
                        bbc_ev = {}
                        for qc in range(4):
                            wt("T", vs, bits_ev[3] if qc == 0 else bbc_ev[qc - 1])
                            nT = op("T", lambda e, qc=qc: e.matmul(
                                PS[:, 0:512], pow2xb[:], bitsb[:, qc * 512:(qc + 1) * 512],
                                start=True, stop=True), inc=True)
                            wt("V", ts, nT)
                            bbc_ev[qc] = op("V", lambda e, qc=qc, hb=hb: e.tensor_copy(
                                bbc[hb][:, qc * 512:(qc + 1) * 512], PS[:, 0:512]), inc=True)
                        wt("T", vs, bbc_ev[3])
                        nT = None
                        for tc in range(16):
                            nT = op("T", lambda e, tc=tc: e.matmul(
                                PS[:, tc:tc + 1], bitsb[:, tc * 128:(tc + 1) * 128], pow2cb[:],
                                start=True, stop=True), inc=(tc == 15))
                        wt("V", ts, nT)
                        nbk = op("V", lambda e, hb=hb: e.tensor_copy(bkT[hb][:], PS[:, 0:16]), inc=True)
                        prep_banks_v["n"] = nbk
                        wmult_prev[0] = nbk
                        return nbk

                    def emit_scores(h, qtg):
                        hb = h % 2
                        beta = 64 * hb
                        ch = h // 2
                        par = (h * 8 + qtg) % 2
                        out = []
                        u2 = (h * 8 + qtg) * 2
                        for qt in range(2):
                            qg = qtg * 256 + qt * 128
                            nW = {}
                            for th in range(2):
                                wt("T", vs, wmult_prev[th])
                                nTs = None
                                for nn in range(2):
                                    sc = th * 2 + nn
                                    nTs = op("T", lambda e, sc=sc, beta=beta, ch=ch, qg=qg: e.matmul(
                                        PS[:, sc * 512:sc * 512 + 512],
                                        qsT[beta:beta + 64, ch * S + qg: ch * S + qg + 128],
                                        qsT[beta:beta + 64, ch * S + sc * 512: ch * S + sc * 512 + 512],
                                        start=True, stop=True), inc=(nn == 1))
                                nW[th] = nTs
                            tile_idx = qtg * 2 + qt
                            cb = coefb[(u2 + qt) % 2]
                            op("V", lambda e, hb=hb, tile_idx=tile_idx, cb=cb: e.tensor_scalar(
                                cb[:], bbc[hb][:],
                                bkT[hb][:, tile_idx:tile_idx + 1], NB2,
                                OP.is_equal, OP.add))
                            for th in range(2):
                                wt("V", ts, nW[th])
                                wt("V", ss, wb_reader_s[qt] if th == 0 else None)
                                wmult_prev[th] = op("V", lambda e, th=th, qt=qt, cb=cb: e.tensor_tensor(
                                    Wb[qt][:, th * 1024:th * 1024 + 1024],
                                    PS[:, th * 1024:th * 1024 + 1024],
                                    cb[:, th * 1024:th * 1024 + 1024], OP.mult), inc=True)
                            ncmb = op("V", lambda e, qt=qt: e.reduce_max(
                                gmx[:, 4 + qt:5 + qt], Wb[qt][:],
                                mybir.AxisListType.X, negate=True), inc=True)
                            out.append(ncmb)
                        return par, out

                    def emit_exp(h, qtg, par, ncmbs):
                        for qt in range(2):
                            slot = par * 2 + qt
                            wt("S", vs, ncmbs[qt])
                            wt("S", ts, transp_last[par])
                            wb_reader_s[qt] = op("S", lambda e, qt=qt, slot=slot: e.activation(
                                Ugrp[:, slot * S:(slot + 1) * S], Wb[qt][:], AF.Exp,
                                bias=gmx[:, 4 + qt:5 + qt]), inc=True)
                            exp_prev[slot] = wb_reader_s[qt]

                    pt_free = {0: (vs, ln1_ready[14][1]), 1: (vs, ln1_ready[15][1])}
                    av_done = {}
                    av_free = {0: w2_evac[126], 1: w2_evac[127]}
                    def emit_finishA(h, qtg):
                        u = h * 8 + qtg
                        par = u % 2
                        AV = PC if par == 0 else AV2
                        st8 = {"last": None}
                        evn = {}

                        def transp_rnd(rnd):
                            bk = rnd % 2
                            Pt = PT if bk == 0 else PT2
                            if rnd == 0:
                                wt("T", ss, exp_prev.get(par * 2 + 0))
                                wt("T", ss, exp_prev.get(par * 2 + 1))
                            if rnd < 2:
                                fb = pt_free[bk]
                                if fb is not None:
                                    wt("T", fb[0], fb[1])
                            nTt = None
                            for bi in range(8):
                                kt = rnd * 4 + bi // 2
                                qt = bi % 2
                                slot = par * 2 + qt
                                nTt = op("T", lambda e, kt=kt, slot=slot, bi=bi, Pt=Pt: e.transpose(
                                    Pt[:, bi * 128:(bi + 1) * 128],
                                    Ugrp[:, slot * S + kt * 128: slot * S + kt * 128 + 128],
                                    identb[:]), inc=(bi == 7))
                            transp_last[par] = nTt
                            wt("S", ts, nTt)
                            nev = op("S", lambda e, rnd=rnd, Pt=Pt: e.activation(
                                UTs[:, rnd * 1024:(rnd + 1) * 1024], Pt[:], AF.Identity), inc=True)
                            pt_free[bk] = (ss, nev)
                            evn[rnd] = nev

                        def av_rnd(rnd):
                            wt("T", ss, evn[rnd])
                            if rnd == 0:
                                wt("T", vs, av_free[par])
                            for kti in range(4):
                                kt = rnd * 4 + kti
                                st8["last"] = op("T", lambda e, kt=kt, h=h, AV=AV: e.matmul(
                                    AV[0:65, 0:256],
                                    vhat[:, (kt * 16 + h) * 65:(kt * 16 + h) * 65 + 65],
                                    UTs[:, kt * 256:(kt + 1) * 256],
                                    start=(kt == 0), stop=(kt == 15)), inc=(kt == 15))

                        transp_rnd(0)
                        transp_rnd(1)
                        av_rnd(0)
                        transp_rnd(2)
                        av_rnd(1)
                        transp_rnd(3)
                        av_rnd(2)
                        av_rnd(3)
                        av_done[u] = st8["last"]

                    store_n = {0: None, 1: None}
                    store_qtg = {}
                    def emit_finishB(h, qtg):
                        u = h * 8 + qtg
                        par = u % 2
                        AV = PC if par == 0 else AV2
                        wt("V", ts, av_done[u])
                        wt("V", dS, store_n[par])
                        nava = op("V", lambda e, AV=AV, par=par: e.tensor_copy(
                            avsb[par][:], AV[0:65, 0:256]), inc=True)
                        av_free[par] = nava
                        wt("Y", vs, nava)
                        store_n[par] = dma(attnscr[0:65, h * S + qtg * 256: h * S + qtg * 256 + 256],
                                           avsb[par][:])
                        if h == 15:
                            store_qtg[qtg] = store_n[par]

                    prep_done = {}
                    wt("T", dG, nG_consts)
                    prep_done[0] = emit_prep(0)
                    units = [(u // 8, u % 8) for u in range(H * 8)]
                    for u, (h, qtg) in enumerate(units):
                        if qtg == 0:
                            wt("T", vs, prep_done[h])
                            if h == 0:
                                wt("T", vs, n_ones)
                                wt("S", vs, n_ones)
                        par, ncmbs = emit_scores(h, qtg)
                        if u >= 1:
                            emit_finishA(*units[u - 1])
                        if u >= 2:
                            emit_finishB(*units[u - 2])
                        emit_exp(h, qtg, par, ncmbs)
                        if qtg == 3 and h + 1 < H:
                            prep_done[h + 1] = emit_prep(h + 1)
                    emit_finishA(*units[-1])
                    emit_finishB(*units[-2])
                    emit_finishB(*units[-1])
                    attn_done_s = cSS.v

            with ExitStack() as c6:
                xn2T = c6.enter_context(nc.sbuf_tensor("xn2T", [128, 8 * S], BF16))
                attnT = [c6.enter_context(nc.sbuf_tensor("attnT0", [65, 16 * 128], BF16)),
                         c6.enter_context(nc.sbuf_tensor("attnT1", [65, 16 * 128], BF16))]
                attn_sc = c6.enter_context(nc.sbuf_tensor("attn_sc", [128, E], F32))
                drec = c6.enter_context(nc.sbuf_tensor("drec", [128, 16], F32))
                wt("V", ss, attn_done_s)
                ascr3 = attnscr.rearrange("p (h q) -> p h q", q=S)
                nava_last = max(av_free[0], av_free[1])

                def load_attnT(tc):
                    wt("Y", dS, store_qtg[tc // 2])
                    b6 = tc % 2
                    a3 = attnT[b6].ap().rearrange("p (h c) -> p h c", c=128)
                    return dma(a3[:, :, :], ascr3[0:65, :, tc * 128:(tc + 1) * 128])

                wt("Y", ts, av_done[127])
                wt("Y", vs, nava_last)
                wt("Y", ss, attn_done_s)
                load_n = {0: load_attnT(0), 1: load_attnT(1)}
                p6pt = dict(pt_free)
                p6k = {"i": 0}
                p6 = {}
                x1_store = {}
                for tc in range(16):
                    b = tc % 2
                    if tc >= 2:
                        wt("Y", vs, p6[tc - 2][0])
                        wt("V", dS, x1_store[tc - 2])
                    nload = dma(xt[b][:], x_h[tc * 128:(tc + 1) * 128, :])
                    wt("T", dS, load_n[tc])
                    nTS = None
                    nTa = None
                    for rr in range(2):
                        bk = p6k["i"] % 2
                        p6k["i"] += 1
                        Pp = PT if bk == 0 else PT2
                        fb = p6pt[bk]
                        if fb is not None:
                            wt("T", fb[0], fb[1])
                        for hh in range(8):
                            nTa = op("T", lambda e, rr=rr, hh=hh, b=b, Pp=Pp: e.transpose(
                                Pp[:, hh * 128:hh * 128 + 65],
                                attnT[b][0:65, (rr * 8 + hh) * 128:(rr * 8 + hh) * 128 + 128],
                                identb[0:65, 0:65]), inc=(hh == 7))
                        wt("V", ts, nTa)
                        pt3 = Pp.ap().rearrange("p (hh c) -> p hh c", c=128)
                        dr3 = drec.ap().rearrange("p (a o) -> p a o", o=1)
                        ndr = op("V", lambda e, rr=rr, pt3=pt3, dr3=dr3: e.reciprocal(
                            dr3[:, rr * 8:rr * 8 + 8, :], pt3[:, :, 64:65]), inc=True)
                        wt("V", vs, ndr)
                        for hh in range(8):
                            h = rr * 8 + hh
                            nTS = op("V", lambda e, h=h, hh=hh, Pp=Pp: e.tensor_scalar(
                                attn_sc[:, h * 64:(h + 1) * 64], Pp[:, hh * 128:hh * 128 + 64],
                                drec[:, h:h + 1], None, OP.mult), inc=(hh == 7))
                        p6pt[bk] = (vs, nTS)
                    if tc + 2 < 16:
                        wt("Y", ts, nTa)
                        load_n[tc + 2] = load_attnT(tc + 2)
                    wt("V", vs, nTS)
                    ln_chain(attn_sc.ap())
                    wt("S", vs, state["lnA_reader_v"])
                    napp = ln_apply(lnA[:], attn_sc[:])
                    wt("V", ss, napp)
                    wt("V", dS, nload)
                    nx1 = op("V", lambda e, b=b: e.tensor_tensor(
                        x1t[b][:], xt[b][:], lnA[:], OP.add), inc=True)
                    state["lnA_reader_v"] = nx1
                    wt("Y", vs, nx1)
                    x1_store[tc] = dma(x1scr[tc * 128:(tc + 1) * 128, :], x1t[b][:])
                    ln_chain(x1t[b].ap())
                    if tc >= 2:
                        wt("S", ts, p6[tc - 2][1])
                    napp2 = ln_apply(xn2t[b][:], x1t[b][:])
                    wt("T", ss, napp2)
                    bk = p6k["i"] % 2
                    p6k["i"] += 1
                    Pp = PT if bk == 0 else PT2
                    fb = p6pt[bk]
                    if fb is not None:
                        wt("T", fb[0], fb[1])
                    nT = None
                    for ec in range(8):
                        nT = op("T", lambda e, b=b, ec=ec, Pp=Pp: e.transpose(
                            Pp[:, ec * 128:(ec + 1) * 128], xn2t[b][:, ec * 128:(ec + 1) * 128],
                            identb[:]), inc=(ec == 7))
                    wt("V", ts, nT)
                    xv2 = xn2T.ap().rearrange("p (ec t) -> p ec t", ec=8)
                    nev = op("V", lambda e, tc=tc, xv2=xv2, Pp=Pp: e.tensor_copy(
                        xv2[:, :, tc * 128:(tc + 1) * 128],
                        Pp.ap().rearrange("p (ec u) -> p ec u", ec=8)), inc=True)
                    p6pt[bk] = (vs, nev)
                    p6[tc] = (nx1, nT, nev, napp2)
                ln2_done_v = p6[15][2]

                with ExitStack() as c7:
                    WcT = [c7.enter_context(nc.sbuf_tensor(f"WcT{i}", [128, 8 * E], BF16)) for i in range(4)]
                    hactb = [c7.enter_context(nc.sbuf_tensor("hact0", [128, 256], BF16)),
                             c7.enter_context(nc.sbuf_tensor("hact1", [128, 256], BF16))]
                    accF = c7.enter_context(nc.sbuf_tensor("accF", [128, 8 * E], F32))

                    wt("T", vs, ln2_done_v)
                    wt("Y", dG, nG_wc)

                    def load_wct(g):
                        buf = WcT[g % 4]
                        n = None
                        for j in range(8):
                            hcol = (j // 2) * 4096 + (4 * g + 2 * (j % 2)) * 64
                            n = dma(buf[:, j * E:(j + 1) * E],
                                    wcbf[0:E, hcol:hcol + 128], transpose=True)
                        return n

                    flush_last = None
                    out_store = {}
                    pair_T = {}
                    wct_n = {}
                    last_gelu = {0: None, 1: None}
                    GPAIRS = 16
                    def pair_groups(p):
                        th = p // 8
                        gp = p % 8
                        return th, 2 * gp, 2 * gp + 1
                    for p in (0, 1):
                        _, g0, g1 = pair_groups(p)
                        load_wct(g0)
                        wct_n[p] = load_wct(g1)
                    for p in range(GPAIRS):
                        th, g0, g1 = pair_groups(p)
                        toff = th * 1024
                        if th == 1 and g0 == 0:
                            wt("V", ss, state["msc_reader_s"])
                        wt("T", dS, wct_n[p])
                        nlast = None
                        for tcB in range(4):
                            wt("T", vs, flush_last)
                            first = {(0, 0): True, (0, 1): True, (1, 0): True, (1, 1): True}
                            seqs = [(g, j) for g in (g0, g1) for j in range(8)]
                            gel = {}
                            def emit_z(i):
                                g, j = seqs[i]
                                par = i % 2
                                Pz = PC if par == 0 else AV2
                                wt("T", ss, last_gelu[par])
                                beta = 64 * (g % 2)
                                ch = g // 2
                                nz = op("T", lambda e, Pz=Pz, beta=beta, ch=ch, j=j, g=g, tcB=tcB, toff=toff: e.matmul(
                                    Pz[:, 0:256],
                                    W2S[beta:beta + 64, ch * 1024 + j * 128: ch * 1024 + (j + 1) * 128],
                                    xn2T[beta:beta + 64, ch * S + toff + tcB * 256: ch * S + toff + (tcB + 1) * 256],
                                    start=True, stop=True), inc=True)
                                wt("S", ts, nz)
                                ng = op("S", lambda e, par=par, Pz=Pz: e.activation(
                                    hactb[par][:], Pz[:, 0:256], AF.Gelu), inc=True)
                                last_gelu[par] = ng
                                gel[i] = ng
                            def emit_cproj(i):
                                nonlocal nlast
                                g, j = seqs[i]
                                par = i % 2
                                wt("T", ss, gel[i])
                                for st in range(2):
                                    Pout = PS.ap()[:, st * 1024:(st + 1) * 1024]
                                    for eh in range(2):
                                        stt = first[(st, eh)]
                                        first[(st, eh)] = False
                                        stop = (i == 15)
                                        nlast = op("T", lambda e, par=par, st=st, eh=eh, g=g, j=j, Pout=Pout, stt=stt, stop=stop: e.matmul(
                                            Pout[:, eh * 512:eh * 512 + 512],
                                            hactb[par][:, st * 128:(st + 1) * 128],
                                            WcT[g % 4][:, j * E + eh * 512: j * E + eh * 512 + 512],
                                            start=stt, stop=stop),
                                            inc=(stop and st == 1 and eh == 1))
                            emit_z(0)
                            for i in range(1, 16):
                                emit_z(i)
                                emit_cproj(i - 1)
                            emit_cproj(15)
                            wt("V", ts, nlast)
                            for st in range(2):
                                Pout = PS.ap()[:, st * 1024:(st + 1) * 1024]
                                col = tcB * 2 + st
                                if g0 == 0:
                                    flush_last = op("V", lambda e, Pout=Pout, col=col: e.tensor_copy(
                                        accF[:, col * E:(col + 1) * E], Pout[:]), inc=True)
                                else:
                                    flush_last = op("V", lambda e, Pout=Pout, col=col: e.tensor_tensor(
                                        accF[:, col * E:(col + 1) * E],
                                        accF[:, col * E:(col + 1) * E], Pout[:], OP.add), inc=True)
                        pair_T[p] = nlast
                        if p + 2 < GPAIRS:
                            wt("Y", ts, pair_T[p])
                            _, ng0, ng1 = pair_groups(p + 2)
                            load_wct(ng0)
                            wct_n[p + 2] = load_wct(ng1)
                        if p % 8 != 7:
                            continue
                        for tci in range(8):
                            tc = th * 8 + tci
                            b = tc % 2
                            if th == 0 and tci < 2:
                                wt("Y", ss, p6[14 + tci][3])
                                wt("Y", dS, x1_store[14 + tci])
                            else:
                                wt("Y", vs, p6.get(("p8", tc - 2)))
                            nload = dma(x1t[b][:], x1scr[tc * 128:(tc + 1) * 128, :])
                            wt("V", vs, flush_last)
                            ln_chain(accF.ap()[:, tci * E:(tci + 1) * E])
                            wt("S", vs, state["lnA_reader_v"])
                            napp = ln_apply(lnA[:], accF[:, tci * E:(tci + 1) * E])
                            wt("V", ss, napp)
                            wt("V", dS, nload)
                            nfin = op("V", lambda e, b=b: e.tensor_tensor(
                                xt[b][:], x1t[b][:], lnA[:], OP.add), inc=True)
                            state["lnA_reader_v"] = nfin
                            p6[("p8", tc)] = nfin
                            wt("Y", vs, nfin)
                            out_store[tc] = dma(out_h[tc * 128:(tc + 1) * 128, :], xt[b][:])
                    wt("Y", dS, cS.v)

        blk = ctx.enter_context(nc.Block())
        blk.tensor(lambda e: [f(e) for f in Qs["T"]])
        blk.vector(lambda e: [f(e) for f in Qs["V"]])
        blk.scalar(lambda e: [f(e) for f in Qs["S"]])
        blk.gpsimd(lambda e: [f(e) for f in Qs["G"]])
        blk.sync(lambda e: [f(e) for f in Qs["Y"]])
    return nc


def host_consts(Wq, bq, Wv, bv, hp, rm, up):
    c = {}
    c["bq_pp"] = np.ascontiguousarray(bq.reshape(8, 128).T.astype(np.float32))
    c["bv_bc"] = np.ascontiguousarray(np.broadcast_to(bv.astype(np.float32), (128, 1024)))
    c["hp_dup"] = np.ascontiguousarray(np.concatenate([hp[0:64], hp[0:64]], axis=0).astype(np.float32))
    c["hp_bias"] = np.ascontiguousarray(hp[64:65, :].T.astype(np.float32))
    pow2 = (2.0 ** np.arange(6)).astype(np.float32)
    c["pow2x"] = np.ascontiguousarray(np.broadcast_to(pow2[:, None], (6, 128)))
    c["pow2c"] = np.ascontiguousarray(pow2[:, None])
    c["ident"] = np.eye(128, dtype=np.float32)
    rm4 = np.zeros((64, 64, 64), dtype=np.float32)
    up4 = np.zeros((64, 64, 128), dtype=np.float32)
    for cc in range(128):
        g, j = cc // 8, cc % 8
        h = j // 2
        jp = j % 2
        rb = 32 * (cc % 2)
        cb = cc // 2
        for u in range(2):
            r = 4 * g + 2 * jp + u
            i0 = 16 * (2 * jp + u)
            rm4[rb + u * 16:rb + (u + 1) * 16, cb, i0:i0 + 16] = rm[h, r].T
            up4[rb + u * 16:rb + (u + 1) * 16, cb, 64 * u:64 * u + 64] = up[h, r].T
    c["rm4"] = np.ascontiguousarray(rm4.reshape(64, 64 * 64))
    c["up4"] = np.ascontiguousarray(up4.reshape(64, 64 * 128))
    return c


def kernel(L0_attn_x, L0_attn_Wq, L0_attn_bq, L0_attn_Wv, L0_attn_bv,
           L0_attn_hyperplanes, L0_ffn_rail_maker, L0_ffn_rail_up_proj, L0_ffn_w_cproj):
    x = np.ascontiguousarray(np.asarray(L0_attn_x, dtype=np.float32))
    shared = host_consts(np.asarray(L0_attn_Wq, np.float32), np.asarray(L0_attn_bq, np.float32),
                         np.asarray(L0_attn_Wv, np.float32), np.asarray(L0_attn_bv, np.float32),
                         np.asarray(L0_attn_hyperplanes, np.float32),
                         np.asarray(L0_ffn_rail_maker, np.float32),
                         np.asarray(L0_ffn_rail_up_proj, np.float32))
    shared["wq"] = np.ascontiguousarray(np.asarray(L0_attn_Wq, np.float32))
    shared["wv"] = np.ascontiguousarray(np.asarray(L0_attn_Wv, np.float32))
    shared["wc"] = np.ascontiguousarray(np.asarray(L0_ffn_w_cproj, np.float32))
    nc = build_program()
    in_maps = []
    for b in range(8):
        m = dict(shared)
        m["x"] = np.ascontiguousarray(x[b])
        in_maps.append(m)
    res = run_bass_kernel_spmd(nc, in_maps, core_ids=list(range(8)))
    out = np.stack([res.results[b]["out"] for b in range(8)], axis=0)
    return out.astype(np.float32)
